# version 19
# baseline (speedup 1.0000x reference)
import numpy as np
import ml_dtypes
from contextlib import ExitStack

import concourse.bass as bass
import concourse.mybir as mybir
from concourse.bass_utils import run_bass_kernel_spmd

F32 = mybir.dt.float32
BF16 = mybir.dt.bfloat16
AF = mybir.ActivationFunctionType
OP = mybir.AluOpType
AX = mybir.AxisListType

NCORE = 8
T = 2048
D = 1024
E = 8
F = 3264
FP = 3328
KD = 8
MF = 26
NT = 16
NQ = 4
EPS = 1e-5
BIG = 1e30
NPASS = E * NQ
NG = NPASS * MF


def build(dbg=False, stop="full", e_run=E):
    LV = {"ln": 0.3, "scan": 0.6, "bout": 0.8, "wonly": 0.9, "attn": 1,
          "routing": 2, "full": 3}[stop]
    nc = bass.Bass("TRN2", target_bir_lowering=False)

    x_d = nc.dram_tensor("x_in", [T, D], F32, kind="ExternalInput").ap()
    aft_d = nc.dram_tensor("a_ft", [128, 8], F32, kind="ExternalInput").ap()
    bft_d = nc.dram_tensor("b_ft", [128, 8], F32, kind="ExternalInput").ap()
    boutB_d = nc.dram_tensor("b_outB", [128, D], F32, kind="ExternalInput").ap()
    gbB_d = nc.dram_tensor("gate_bB", [128, 8], F32, kind="ExternalInput").ap()
    io8_d = nc.dram_tensor("iota8", [128, 8], F32, kind="ExternalInput").ap()
    io16_d = nc.dram_tensor("iota16", [128, 8], F32, kind="ExternalInput").ap()
    gpbP_d = nc.dram_tensor("gp_bP", [E, 128, MF], F32, kind="ExternalInput").ap()
    upbP_d = nc.dram_tensor("up_bP", [E, 128, MF], F32, kind="ExternalInput").ap()
    woT_d = nc.dram_tensor("woT", [D, D], F32, kind="ExternalInput").ap()
    gateT_d = nc.dram_tensor("gateT", [D, 8], F32, kind="ExternalInput").ap()
    dnb_d = nc.dram_tensor("down_b", [E, D], F32, kind="ExternalInput").ap()
    gpT_d = nc.dram_tensor("gpT", [E, D, FP], F32, kind="ExternalInput").ap()
    upT_d = nc.dram_tensor("upT", [E, D, FP], F32, kind="ExternalInput").ap()
    dnT_d = nc.dram_tensor("dnT", [E, FP, D], F32, kind="ExternalInput").ap()
    out_d = nc.dram_tensor("out", [T, D], F32, kind="ExternalOutput").ap()
    if dbg:
        dbg_xa = nc.dram_tensor("dbg_xa", [T, D], F32, kind="ExternalOutput").ap()
        dbg_comb = nc.dram_tensor("dbg_comb", [128, NT, 8], F32,
                                  kind="ExternalOutput").ap()
        dbg_y = nc.dram_tensor("dbg_y", [128, KD, T], BF16,
                               kind="ExternalOutput").ap()

    ctx = ExitStack()
    sb = lambda name, shape, dt: ctx.enter_context(nc.sbuf_tensor(name, shape, dt))
    psA = [ctx.enter_context(nc.psum_tensor(f"psA{i}", [128, 512], F32))
           for i in range(2)]
    psB = [ctx.enter_context(nc.psum_tensor(f"psB{i}", [128, 512], F32))
           for i in range(2)]
    gp_ps = [ctx.enter_context(nc.psum_tensor(f"gpps{i}", [128, 512], F32))
             for i in range(2)]
    up_ps = [ctx.enter_context(nc.psum_tensor(f"upps{i}", [128, 512], F32))
             for i in range(2)]
    d_banks = [psB[0], psB[1], psA[0], psA[1]]
    psC = [psA[0][:].bitcast(BF16), psA[1][:].bitcast(BF16)]
    psCb = psC + [gp_ps[0][:].bitcast(BF16), gp_ps[1][:].bitcast(BF16)]

    x_sb = sb("x_sb", [128, NT, D], F32)
    xn2T = sb("xn2T", [128, KD, T], BF16)
    hq = sb("hq", [128, 13, 1024], BF16)
    dnT_sb = sb("dnT_sb", [128, MF, D], BF16)
    gp_sb = sb("gp_sb", [128, 2, KD, 256], BF16)
    up_sb = sb("up_sb", [128, 2, KD, 256], BF16)
    aft = sb("aft", [128, 8], F32)
    ah = sb("ah", [128, 8], F32)
    bft = sb("bft", [128, 8], F32)
    gbB = sb("gbB", [128, 8], F32)
    io8 = sb("io8", [128, 8], F32)
    io16 = sb("io16", [128, 8], F32)
    gpbP = sb("gpbP", [128, E, MF], F32)
    upbP = sb("upbP", [128, E, MF], F32)
    dnb_sb = sb("dnb_sb", [8, D], BF16)
    gateT_sb = sb("gateT_sb", [128, KD, 8], BF16)
    identf = sb("identf", [128, 128], F32)
    identb = sb("identb", [128, 128], BF16)
    t1b = sb("t1b", [128, 2, 512], F32)
    combT_sb2 = sb("combT_sb2", [8, T], BF16)
    comb = sb("comb", [128, NT, 8], F32)
    comb_bf = sb("comb_bf", [128, NT, 8], BF16)
    st1 = {k: sb("st1_" + k, [128, NT], F32) for k in
           ["sum", "sq", "m", "msq", "var", "rv", "rstd", "nmr"]}
    st2 = {k: sb("st2_" + k, [128, NT], F32) for k in
           ["sum", "sq", "m", "msq", "var", "rv", "rstd", "nmr"]}
    rw = [sb("rw%d" % i, [128, NT, 8], F32) for i in range(4)]
    rtn = {k: sb("rtn_" + k, [128, NT, 1], F32) for k in
           ["nmx", "z", "rz", "m1", "m2", "idx", "den", "inv"]}

    r1f = hq[:].rearrange("p a b -> p (a b)").bitcast(F32)
    r1b = hq[:].rearrange("p a b -> p (a b)")
    bx_pp = [r1f[:, 0:2048], r1f[:, 2048:4096]]
    junk = r1f[:, 4608:5632]
    xn2_t = [r1b[:, 11264:12288], r1b[:, 12288:13312]]
    xn2T_f = xn2T[:].rearrange("p k t -> p (k t)").bitcast(F32)
    xn1_q = [xn2T_f[:, 512 * j:512 * (j + 1)] for j in range(16)]
    r2f = dnT_sb[:].rearrange("p a b -> p (a b)").bitcast(F32)
    r2b = dnT_sb[:].rearrange("p a b -> p (a b)")
    w_oT = r2b[:, 0:8192].rearrange("p (k e) -> p k e", k=KD)
    ybf = r2b[:, 8192:24576].rearrange("p (k t) -> p k t", k=KD)
    b_outB = r2f[:, 12288:13312]
    lg8 = r2f[0:8, 4096:6144]
    combT = combT_sb2[:]

    sem_names = ["s_x", "s_xg", "s_xv", "s_xs", "s_c", "s_cw", "s_bo", "s_wo",
                 "s_gt", "s_db", "s_dn",
                 "s_gpA", "s_gpB", "g_id", "g_ln", "g_bo",
                 "t_tr", "t_wo", "t_xtr", "t_gate", "t_ltr", "t_ctr",
                 "t_r8", "t_h", "t_d",
                 "v_ln", "v_bxe", "v_ybf", "v_woe", "v_xtile", "v_xne",
                 "v_gevac", "v_ltre", "v_comb", "v_ctre", "v_r8e",
                 "v_stt", "v_de", "v_ch", "v_eps1", "v_eps2", "v_lgsh",
                 "sc_sq", "sc_rstd", "sc_sq2", "sc_rstd2", "sc_exp", "sc_ah",
                 "sc_t1", "sc_h", "c_gu", "s_st", "s_dbg"]
    S = {}
    for n in sem_names:
        S[n] = ctx.enter_context(nc.semaphore(n))
    s_gp = [S["s_gpA"], S["s_gpB"]]

    DN_CH = [(0, 7), (7, 13), (13, 20), (20, 26)]
    NSP = E * 4

    def slice_mrange(mp):
        return (2 * mp, min(2 * mp + 2, 13))

    def slice_consumed_th(sidx):
        sp2, mp2 = sidx // 7, sidx % 7
        mlast = min(2 * mp2 + 1, 12)
        return (sp2 * 13 + mlast) * 2 + 2

    xq_d = x_d.rearrange("(t p) d -> p t d", p=128)

    with nc.Block() as block:

        @block.sync
        def _(sy):
            for src, dst in [(aft_d, aft), (bft_d, bft), (gbB_d, gbB),
                             (io8_d, io8), (io16_d, io16)]:
                sy.dma_start(out=dst[:], in_=src).then_inc(S["s_c"], 16)
            sy.dma_start(out=x_sb[:, 0:2, :], in_=xq_d[:, 0:2, :]
                         ).then_inc(S["s_x"], 16)
            sy.dma_start(out=x_sb[:, 6:8, :], in_=xq_d[:, 6:8, :]
                         ).then_inc(S["s_x"], 16)
            sy.dma_start(out=x_sb[:, 12:14, :], in_=xq_d[:, 12:14, :]
                         ).then_inc(S["s_x"], 16)
            sy.dma_start(out=gpbP[:], in_=gpbP_d.rearrange("e p m -> p e m")
                         ).then_inc(S["s_cw"], 16)
            sy.dma_start(out=upbP[:], in_=upbP_d.rearrange("e p m -> p e m")
                         ).then_inc(S["s_cw"], 16)
            sy.dma_start(out=b_outB, in_=boutB_d).then_inc(S["s_bo"], 16)
            if dbg:
                sy.wait_ge(S["v_ybf"], 4 * KD)
                sy.dma_start(out=dbg_y, in_=ybf).then_inc(S["s_dbg"], 16)
                sy.wait_ge(S["v_woe"], 32)
                sy.dma_start(out=dbg_xa.rearrange("(t p) d -> p t d", p=128),
                             in_=x_sb[:]).then_inc(S["s_dbg"], 16)
                sy.wait_ge(S["v_comb"], 1)
                sy.dma_start(out=dbg_comb, in_=comb[:]).then_inc(S["s_dbg"], 16)
            for g in range(NT):
                if LV < 3:
                    if LV == 0.3:
                        sy.wait_ge(S["sc_rstd"], 4)
                    elif LV == 0.6:
                        sy.wait_ge(S["v_ybf"], 4 * KD)
                    elif LV == 0.8:
                        sy.wait_ge(S["v_ch"], 1)
                    elif LV == 0.9:
                        sy.wait_ge(S["t_wo"], 32)
                    elif LV == 1:
                        sy.wait_ge(S["v_woe"], 32)
                    elif LV == 2:
                        sy.wait_ge(S["v_r8e"], 32)
                    sy.dma_start(
                        out=out_d.rearrange("(t p) d -> p t d", p=128)[:, g, :],
                        in_=x_sb[:, g, :]).then_inc(S["s_st"], 16)
                    continue
                spf = 4 * (e_run - 1) + 2 + g // 8
                for dh in range(2):
                    sy.wait_ge(S["v_de"], spf * 16 + (g % 8) * 2 + dh + 1)
                    sy.dma_start(
                        out=out_d.rearrange("(t p) d -> p t d", p=128)[
                            :, g, dh * 512:(dh + 1) * 512],
                        in_=x_sb[:, g, dh * 512:(dh + 1) * 512]
                        ).then_inc(S["s_st"], 16)
            sy.wait_ge(S["s_st"], 16 * NT * (1 if LV < 3 else 2))

        @block.gpsimd
        def _(g):
            g.memset(identf[:], 0.0).then_inc(S["g_id"], 1)
            g.wait_ge(S["g_id"], 1)
            g.affine_select(out=identf[:], in_=identf[:], compare_op=OP.not_equal,
                            fill=1.0, base=0, pattern=[[-1, 128]],
                            channel_multiplier=1).then_inc(S["g_id"], 1)
            g.memset(identb[:], 0.0).then_inc(S["g_id"], 1)
            g.wait_ge(S["g_id"], 3)
            g.affine_select(out=identb[:], in_=identb[:], compare_op=OP.not_equal,
                            fill=1.0, base=0, pattern=[[-1, 128]],
                            channel_multiplier=1).then_inc(S["g_id"], 1)
            g.dma_start(out=x_sb[:, 4:6, :], in_=xq_d[:, 4:6, :]
                        ).then_inc(S["s_xg"], 16)
            g.dma_start(out=x_sb[:, 10:12, :], in_=xq_d[:, 10:12, :]
                        ).then_inc(S["s_xg"], 16)
            g.dma_start(out=gateT_sb[:],
                        in_=gateT_d.rearrange("(k p) e -> p k e", p=128)
                        ).then_inc(S["s_gt"], 16)
            g.dma_start(out=w_oT, in_=woT_d.rearrange("(k p) e -> p k e", p=128)
                        ).then_inc(S["s_wo"], 16)
            g.dma_start(out=dnb_sb[:], in_=dnb_d).then_inc(S["s_db"], 16)

            def emit_slice(sp, mp):
                e, fh = sp // 4, (sp // 2) % 2
                sidx = sp * 7 + mp
                buf = sidx % 2
                if sidx >= 2:
                    g.wait_ge(S["t_h"], slice_consumed_th(sidx - 2))
                m0, m1 = slice_mrange(mp)
                c0 = (fh * 13 + m0) * 128
                c1 = (fh * 13 + m1) * 128
                w = c1 - c0
                g.dma_start(
                    out=gp_sb[:, buf, :, 0:w],
                    in_=gpT_d[e, :, c0:c1].rearrange(
                        "(k p) f -> p k f", p=128)).then_inc(s_gp[buf], 16)
                g.dma_start(
                    out=up_sb[:, buf, :, 0:w],
                    in_=upT_d[e, :, c0:c1].rearrange(
                        "(k p) f -> p k f", p=128)).then_inc(s_gp[buf], 16)

            def emit_dnt(e, ci):
                if e == 0:
                    if ci == 1:
                        g.wait_ge(S["t_ltr"], NT)
                else:
                    g.wait_ge(S["t_d"],
                              64 * (e - 1) + (32 if ci in (0, 1) else 64))
                k0, k1 = DN_CH[ci]
                g.dma_start(
                    out=dnT_sb[:, k0:k1, :],
                    in_=dnT_d[e, k0 * 128:k1 * 128, :].rearrange(
                        "(k p) d -> p k d", p=128)).then_inc(S["s_dn"], 16)

            if LV >= 3 and e_run > 0:
                emit_slice(0, 0)
                emit_slice(0, 1)

            for e in range(e_run if LV >= 3 else 0):
                sp0 = e * 4
                for mp in range(2 if e == 0 else 0, 4):
                    emit_slice(sp0, mp)
                if e == 0:
                    g.wait_ge(S["v_woe"], 32)
                    if dbg:
                        g.wait_ge(S["s_dbg"], 48)
                emit_dnt(e, 0)
                emit_slice(sp0, 4)
                emit_dnt(e, 1)
                emit_slice(sp0, 5)
                emit_dnt(e, 2)
                emit_slice(sp0, 6)
                emit_dnt(e, 3)
                for sp in range(sp0 + 1, sp0 + 4):
                    for mp in range(7):
                        emit_slice(sp, mp)

        @block.tensor
        def _(t):
            t.wait_ge(S["g_id"], 4)

            def trbank(g):
                return (psA[g % 2] if g >= 28
                        else [psA[0], psA[1], psB[0], psB[1]][g % 4])
            for g in range(32 if LV >= 0.6 else 0):
                for p2 in range(2):
                    pr = 2 * g + p2
                    c = (2 * pr) // 16
                    t.wait_ge(S["v_ln"], (c // 4) * 16 + (2 * pr) % 16 + 2)
                    if p2 == 0:
                        if g >= 28:
                            t.wait_ge(S["v_bxe"], g - 1)
                        elif g >= 4:
                            t.wait_ge(S["v_bxe"], g - 3)
                    for j in range(2):
                        a = 2 * pr + j
                        ins = t.transpose(
                            trbank(g)[:, (a % 4) * 128:(a % 4 + 1) * 128],
                            xn1_q[a % 16][:, (c % 4) * 128:(c % 4 + 1) * 128],
                            identf[:])
                    ins.then_inc(S["t_tr"], 1)
            if LV < 0.9:
                return
            t.wait_ge(S["s_wo"], 16)
            t.wait_ge(S["v_bxe"], 28)
            for m in range(NT):
                for eh in range(2):
                    widx = m * 2 + eh
                    if widx >= 2:
                        t.wait_ge(S["v_woe"] if LV >= 1 else S["t_wo"], widx - 1)
                    for k in range(KD):
                        if widx < 6:
                            t.wait_ge(S["v_ybf"], 4 * (k + 1))
                        ins = t.matmul(psB[widx % 2][:],
                                       ybf[:, k, m * 128:(m + 1) * 128],
                                       w_oT[:, k, eh * 512:(eh + 1) * 512],
                                       start=(k == 0), stop=(k == KD - 1))
                    ins.then_inc(S["t_wo"], 1)
            if LV < 2:
                return
            t.wait_ge(S["s_gt"], 16)
            t.wait_ge(S["v_woe"], 32)
            for tt in range(NT):
                t.wait_ge(S["v_xtile"], tt + 1)
                if tt >= 4:
                    t.wait_ge(S["v_xne"], tt - 3)
                for c in range(KD):
                    ins = t.transpose(psCb[tt % 4][:, c * 128:(c + 1) * 128],
                                      xn2_t[tt % 2][:, c * 128:(c + 1) * 128],
                                      identb[:])
                ins.then_inc(S["t_xtr"], 1)
                if tt % 4 == 3:
                    tq = tt // 4
                    t.wait_ge(S["v_xne"], 4 * tq + 4)
                    if tq >= 2:
                        t.wait_ge(S["v_gevac"], tq - 1)
                    for k in range(KD):
                        ins = t.matmul(psB[tq % 2][0:8, :], gateT_sb[:, k, :],
                                       xn2T[:, k, tq * 512:(tq + 1) * 512],
                                       start=(k == 0), stop=(k == KD - 1))
                    ins.then_inc(S["t_gate"], 1)
            t.wait_ge(S["v_xne"], NT)
            t.wait_ge(S["v_bxe"], 32)
            for g in range(NT):
                t.wait_ge(S["v_gevac"], g // 4 + 1)
                if g >= 2:
                    t.wait_ge(S["v_ltre"], g - 1)
                t.transpose(psA[g % 2][:, 0:8], lg8[:, g * 128:(g + 1) * 128],
                            identf[0:8, 0:8]).then_inc(S["t_ltr"], 1)

            def emit_ctr_r8():
                t.wait_ge(S["v_comb"], 1)
                t.wait_ge(S["v_ltre"], NT)
                for g in range(NT):
                    if g >= 2:
                        t.wait_ge(S["v_ctre"], g - 1)
                    t.transpose(psC[g % 2][0:8, 0:128], comb_bf[:, g, :],
                                identb[:]).then_inc(S["t_ctr"], 1)
                t.wait_ge(S["v_ctre"], NT)
                t.wait_ge(S["s_db"], 16)
                for g in range(NT):
                    for dh in range(2):
                        ridx = g * 2 + dh
                        if ridx >= 2:
                            t.wait_ge(S["v_r8e"], ridx - 1)
                        t.matmul(psA[ridx % 2][:],
                                 combT[:, g * 128:(g + 1) * 128],
                                 dnb_sb[:, dh * 512:(dh + 1) * 512],
                                 start=True, stop=True).then_inc(S["t_r8"], 1)
            if LV == 2:
                emit_ctr_r8()
            for sp in range(NSP if LV >= 3 else 0):
                if sp >= e_run * 4:
                    break
                e, th, fh = sp // 4, sp % 2, (sp // 2) % 2
                for m in range(13):
                    mg = sp * 13 + m
                    mp = m // 2
                    sidx = sp * 7 + mp
                    buf = sidx % 2
                    if m % 2 == 0:
                        t.wait_ge(s_gp[buf], 32 * (sidx // 2 + 1))
                    fo = (m % 2) * 128
                    for tq in range(2):
                        u = mg * 2 + tq
                        if u >= 2:
                            t.wait_ge(S["v_stt"], u - 1)
                        for k in range(KD):
                            t.matmul(gp_ps[tq][:],
                                     gp_sb[:, buf, k, fo:fo + 128],
                                     xn2T[:, k, (th * 2 + tq) * 512:
                                          (th * 2 + tq + 1) * 512],
                                     start=(k == 0), stop=(k == KD - 1))
                        for k in range(KD):
                            ins = t.matmul(up_ps[tq][:],
                                           up_sb[:, buf, k, fo:fo + 128],
                                           xn2T[:, k, (th * 2 + tq) * 512:
                                                (th * 2 + tq + 1) * 512],
                                           start=(k == 0), stop=(k == KD - 1))
                        ins.then_inc(S["t_h"], 1)
                if sp == 0:
                    emit_ctr_r8()
                    t.wait_ge(S["v_r8e"], 32)
                    t.wait_ge(S["v_gevac"], NQ)
                if sp % 4 == 0:
                    t.wait_ge(S["s_dn"], 64 * e + 32)
                elif sp % 4 == 2:
                    t.wait_ge(S["s_dn"], 64 * e + 64)
                for tc in range(8):
                    didx = sp * 16 + tc * 2
                    if didx >= 4:
                        t.wait_ge(S["v_de"], didx - 2)
                    bankA = d_banks[didx % 4]
                    bankB = d_banks[(didx + 1) % 4]
                    for k in range(13):
                        if tc == 0 and k % 2 == 0:
                            t.wait_ge(S["sc_h"], 26 * sp + min(2 * k + 4, 26))
                        insA = t.matmul(bankA[:],
                                        hq[:, k, tc * 128:(tc + 1) * 128],
                                        dnT_sb[:, fh * 13 + k, 0:512],
                                        start=(k == 0), stop=(k == 12))
                    insA.then_inc(S["t_d"], 1)
                    for k in range(13):
                        insB = t.matmul(bankB[:],
                                        hq[:, k, tc * 128:(tc + 1) * 128],
                                        dnT_sb[:, fh * 13 + k, 512:1024],
                                        start=(k == 0), stop=(k == 12))
                    insB.then_inc(S["t_d"], 1)

        @block.vector
        def _(v):
            vch = [0]

            def vsync(ins):
                vch[0] += 1
                ins.then_inc(S["v_ch"], 1)
                v.wait_ge(S["v_ch"], vch[0])

            v.wait_ge(S["s_c"], 80)
            v.wait_ge(S["sc_ah"], 1)

            def stats1_q(q):
                sl = np.s_[:, 4 * q:4 * q + 4]
                v.wait_ge(S["sc_sq"], 8 * (q + 1))
                vsync(v.tensor_scalar(st1["m"][sl], st1["sum"][sl], -1.0 / D,
                                      None, OP.mult))
                vsync(v.tensor_tensor(st1["msq"][sl], st1["m"][sl],
                                      st1["m"][sl], OP.mult))
                vsync(v.scalar_tensor_tensor(st1["var"][sl], st1["sq"][sl],
                                             1.0 / D, st1["msq"][sl],
                                             OP.mult, OP.subtract))
                vsync(v.tensor_scalar(st1["var"][sl], st1["var"][sl], EPS,
                                      None, OP.add))
                v.reciprocal(st1["rv"][sl], st1["var"][sl]
                             ).then_inc(S["v_eps1"], 1)

            if LV < 0.6:
                for q in range(4):
                    stats1_q(q)
                return

            trbanksv = [psA[0], psA[1], psB[0], psB[1]]

            def scan_piece(c, q):
                init = 0.0 if q == 0 else ybf[:, c, 512 * q - 1:512 * q]
                v.tensor_tensor_scan(ybf[:, c, 512 * q:512 * (q + 1)],
                                     ah[:, c:c + 1].to_broadcast([128, 512]),
                                     bx_pp[c % 2][:, 512 * q:512 * (q + 1)],
                                     init, OP.mult,
                                     OP.add).then_inc(S["v_ybf"], 1)

            def emit_evac(pg):
                c = pg // 4
                bank = (psA[pg % 2] if pg >= 28 else trbanksv[pg % 4])
                v.wait_ge(S["t_tr"], 2 * pg + 2)
                v.tensor_scalar(bx_pp[c % 2][:, (pg % 4) * 512:
                                             (pg % 4 + 1) * 512],
                                bank[:, 0:512], bft[:, c:c + 1],
                                None, OP.mult).then_inc(S["v_bxe"], 1)
                scan_piece(c, pg % 4)

            for pg in range(32):
                if pg < 4:
                    stats1_q(pg)
                emit_evac(pg)
            if LV < 0.8:
                return
            v.wait_ge(S["s_bo"], 16)
            for tt in range(NT):
                ins = v.tensor_tensor(x_sb[:, tt, :], x_sb[:, tt, :], b_outB,
                                      OP.add)
            vsync(ins)
            if LV < 1:
                return
            for widx in range(32):
                v.wait_ge(S["t_wo"], widx + 1)
                m, eh = widx // 2, widx % 2
                v.tensor_tensor(x_sb[:, m, eh * 512:(eh + 1) * 512],
                                x_sb[:, m, eh * 512:(eh + 1) * 512],
                                psB[widx % 2][:], OP.add).then_inc(S["v_woe"], 1)
            if LV < 2:
                return

            def stats2_q(q):
                sl = np.s_[:, 4 * q:4 * q + 4]
                v.wait_ge(S["sc_sq2"], 8 * (q + 1))
                vsync(v.tensor_scalar(st2["m"][sl], st2["sum"][sl], -1.0 / D,
                                      None, OP.mult))
                vsync(v.tensor_tensor(st2["msq"][sl], st2["m"][sl],
                                      st2["m"][sl], OP.mult))
                vsync(v.scalar_tensor_tensor(st2["var"][sl], st2["sq"][sl],
                                             1.0 / D, st2["msq"][sl],
                                             OP.mult, OP.subtract))
                vsync(v.tensor_scalar(st2["var"][sl], st2["var"][sl], EPS,
                                      None, OP.add))
                v.reciprocal(st2["rv"][sl], st2["var"][sl]
                             ).then_inc(S["v_eps2"], 1)

            def gevac(tq):
                v.wait_ge(S["t_gate"], tq + 1)
                v.tensor_copy(lg8[:, tq * 512:(tq + 1) * 512],
                              psB[tq % 2][0:8, :]).then_inc(S["v_gevac"], 1)

            def xn2_copy(pt):
                v.wait_ge(S["t_xtr"], pt + 1)
                v.tensor_copy(xn2T[:, :, pt * 128:(pt + 1) * 128],
                              psCb[pt % 4].rearrange("p (k f) -> p k f", k=KD)
                              ).then_inc(S["v_xne"], 1)
                if pt >= 5 and (pt - 5) % 4 == 0 and pt <= 13:
                    gevac((pt - 5) // 4)

            for tt in range(NT):
                if tt % 4 == 0:
                    stats2_q(tt // 4)
                v.wait_ge(S["sc_rstd2"], tt // 4 + 1)
                if tt >= 2:
                    v.wait_ge(S["t_xtr"], tt - 1)
                v.tensor_scalar(xn2_t[tt % 2], x_sb[:, tt, :],
                                st2["m"][:, tt:tt + 1], st2["rstd"][:, tt:tt + 1],
                                OP.add, OP.mult).then_inc(S["v_xtile"], 1)
                if tt >= 1:
                    xn2_copy(tt - 1)
            xn2_copy(NT - 1)
            gevac(3)
            for g in range(NT):
                v.wait_ge(S["t_ltr"], g + 1)
                v.tensor_tensor(rw[0][:, g, :], psA[g % 2][:, 0:8], gbB[:],
                                OP.add).then_inc(S["v_ltre"], 1)
            v.wait_ge(S["v_ltre"], NT)
            WB = [128, NT, 8]
            io8B = io8[:, None, :].to_broadcast(WB)
            io16B = io16[:, None, :].to_broadcast(WB)
            vsync(v.tensor_reduce(rtn["nmx"][:], rw[0][:], axis=AX.X, op=OP.max,
                                  negate=True))
            v.tensor_tensor(rw[1][:], rw[0][:],
                            rtn["nmx"][:].to_broadcast(WB),
                            OP.add).then_inc(S["v_lgsh"], 1)
            v.wait_ge(S["sc_exp"], 1)
            vsync(v.tensor_reduce(rtn["z"][:], rw[2][:], axis=AX.X, op=OP.add))
            vsync(v.reciprocal(rtn["rz"][:], rtn["z"][:]))
            vsync(v.tensor_tensor(rw[1][:], rw[2][:],
                                  rtn["rz"][:].to_broadcast(WB), OP.mult))
            vsync(v.tensor_reduce(rtn["m1"][:], rw[1][:], axis=AX.X, op=OP.max))
            vsync(v.tensor_tensor(rw[0][:], rw[1][:],
                                  rtn["m1"][:].to_broadcast(WB),
                                  OP.is_equal))
            vsync(v.scalar_tensor_tensor(rw[3][:], rw[0][:], -16.0,
                                         io16B, OP.mult, OP.add))
            vsync(v.tensor_reduce(rtn["idx"][:], rw[3][:], axis=AX.X,
                                  op=OP.min))
            vsync(v.tensor_tensor(rw[0][:], io8B,
                                  rtn["idx"][:].to_broadcast(WB),
                                  OP.is_equal))
            vsync(v.scalar_tensor_tensor(rw[3][:], rw[0][:], -BIG,
                                         rw[1][:], OP.mult, OP.add))
            vsync(v.tensor_reduce(rtn["m2"][:], rw[3][:], axis=AX.X, op=OP.max))
            vsync(v.tensor_tensor(rw[2][:], rw[3][:],
                                  rtn["m2"][:].to_broadcast(WB),
                                  OP.is_equal))
            vsync(v.scalar_tensor_tensor(rw[1][:], rw[2][:], -16.0,
                                         io16B, OP.mult, OP.add))
            vsync(v.tensor_reduce(rtn["idx"][:], rw[1][:], axis=AX.X,
                                  op=OP.min))
            vsync(v.tensor_tensor(rw[2][:], io8B,
                                  rtn["idx"][:].to_broadcast(WB),
                                  OP.is_equal))
            vsync(v.tensor_tensor(rtn["den"][:], rtn["m1"][:], rtn["m2"][:], OP.add))
            vsync(v.tensor_scalar(rtn["den"][:], rtn["den"][:], 1e-20, None, OP.add))
            vsync(v.reciprocal(rtn["inv"][:], rtn["den"][:]))
            vsync(v.tensor_tensor(rw[1][:], rw[0][:],
                                  rtn["m1"][:].to_broadcast(WB), OP.mult))
            vsync(v.tensor_tensor(rw[2][:], rw[2][:],
                                  rtn["m2"][:].to_broadcast(WB), OP.mult))
            vsync(v.tensor_tensor(comb[:], rw[1][:], rw[2][:], OP.add))
            vsync(v.tensor_tensor(comb[:], comb[:],
                                  rtn["inv"][:].to_broadcast(WB), OP.mult))
            v.tensor_copy(comb_bf[:], comb[:]).then_inc(S["v_comb"], 1)

            def emit_ctr_r8_evacs():
                for g in range(NT):
                    v.wait_ge(S["t_ctr"], g + 1)
                    v.tensor_copy(combT[:, g * 128:(g + 1) * 128],
                                  psC[g % 2][0:8, 0:128]).then_inc(S["v_ctre"], 1)
                for ridx in range(32):
                    v.wait_ge(S["t_r8"], ridx + 1)
                    gg, dh = ridx // 2, ridx % 2
                    v.tensor_tensor(x_sb[:, gg, dh * 512:(dh + 1) * 512],
                                    x_sb[:, gg, dh * 512:(dh + 1) * 512],
                                    psA[ridx % 2][:], OP.add).then_inc(S["v_r8e"], 1)
            if LV == 2:
                emit_ctr_r8_evacs()
            if LV >= 3 and e_run > 0:
                v.wait_ge(S["s_cw"], 32)
            for sp in range(NSP if LV >= 3 else 0):
                if sp >= e_run * 4:
                    break
                e, th, fh = sp // 4, sp % 2, (sp // 2) % 2
                for m in range(13):
                    for tq in range(2):
                        u = (sp * 13 + m) * 2 + tq
                        v.wait_ge(S["sc_t1"], u + 1)
                        v.wait_ge(S["t_h"], u + 1)
                        if u >= 2:
                            v.wait_ge(S["sc_h"], u - 1)
                        v.scalar_tensor_tensor(
                            t1b[:, u % 2, :], up_ps[tq][:],
                            upbP[:, e, (fh * 13 + m):(fh * 13 + m) + 1],
                            t1b[:, u % 2, :],
                            OP.add, OP.mult).then_inc(S["v_stt"], 1)
                if sp == 0:
                    emit_ctr_r8_evacs()
                for tc in range(8):
                    for dh in range(2):
                        didx = sp * 16 + tc * 2 + dh
                        v.wait_ge(S["t_d"], didx + 1)
                        gt = th * 8 + tc
                        v.scalar_tensor_tensor(
                            x_sb[:, gt, dh * 512:(dh + 1) * 512],
                            d_banks[didx % 4][:],
                            comb[:, gt, e:e + 1],
                            x_sb[:, gt, dh * 512:(dh + 1) * 512],
                            OP.mult, OP.add).then_inc(S["v_de"], 1)

        @block.scalar
        def _(s):
            s.dma_start(out=x_sb[:, 2:4, :], in_=xq_d[:, 2:4, :]
                        ).then_inc(S["s_xs"], 16)
            s.dma_start(out=x_sb[:, 8:10, :], in_=xq_d[:, 8:10, :]
                        ).then_inc(S["s_xs"], 16)
            s.dma_start(out=x_sb[:, 14:16, :], in_=xq_d[:, 14:16, :]
                        ).then_inc(S["s_xs"], 16)
            s.wait_ge(S["s_c"], 80)
            s.activation(out=ah[:], in_=aft[:], func=AF.Tanh).then_inc(S["sc_ah"], 1)
            xw = {0: ("s_x", 16), 2: ("s_xs", 16), 4: ("s_xg", 16),
                  6: ("s_x", 32), 8: ("s_xs", 32), 10: ("s_xg", 32),
                  12: ("s_x", 48), 14: ("s_xs", 48)}
            def apply1(cp, tt):
                if cp >= 1 and tt % 2 == 0:
                    p = ((4 * cp - 1) * 16 + tt) // 2
                    s.wait_ge(S["t_tr"], p + 1)
                s.activation(out=xn1_q[tt],
                             in_=x_sb[:, tt, 512 * cp:512 * (cp + 1)],
                             func=AF.Identity,
                             scale=st1["rstd"][:, tt:tt + 1],
                             bias=st1["nmr"][:, tt:tt + 1]
                             ).then_inc(S["v_ln"], 1)

            for q in range(4):
                for tt in range(4 * q, 4 * q + 4):
                    if tt % 2 == 0:
                        sm, cnt = xw[tt]
                        s.wait_ge(S[sm], cnt)
                    s.activation(out=junk, in_=x_sb[:, tt, :], func=AF.Square,
                                 accum_out=st1["sq"][:, tt:tt + 1]
                                 ).then_inc(S["sc_sq"], 1)
                    s.activation(out=junk, in_=x_sb[:, tt, :], func=AF.Identity,
                                 accum_out=st1["sum"][:, tt:tt + 1]
                                 ).then_inc(S["sc_sq"], 1)
                s.wait_ge(S["v_eps1"], q + 1)
                s.activation(out=st1["rstd"][:, 4 * q:4 * q + 4],
                             in_=st1["rv"][:, 4 * q:4 * q + 4],
                             func=AF.Sqrt).then_inc(S["sc_rstd"], 1)
                if LV >= 0.6:
                    for tt in range(4 * q, 4 * q + 4):
                        s.activation(out=st1["nmr"][:, tt:tt + 1],
                                     in_=st1["m"][:, tt:tt + 1],
                                     func=AF.Identity,
                                     scale=st1["rstd"][:, tt:tt + 1])
                    for tt in range(4 * q, 4 * q + 4):
                        apply1(0, tt)
            if LV >= 0.6:
                for tt in range(NT):
                    apply1(1, tt)
            if LV < 2:
                return
            for tt in range(NT):
                s.wait_ge(S["v_woe"], 2 * tt + 2)
                s.activation(out=junk, in_=x_sb[:, tt, :], func=AF.Square,
                             accum_out=st2["sq"][:, tt:tt + 1]
                             ).then_inc(S["sc_sq2"], 1)
                s.activation(out=junk, in_=x_sb[:, tt, :], func=AF.Identity,
                             accum_out=st2["sum"][:, tt:tt + 1]
                             ).then_inc(S["sc_sq2"], 1)
                if tt % 4 == 3:
                    q = tt // 4
                    s.wait_ge(S["v_eps2"], q + 1)
                    s.activation(out=st2["rstd"][:, 4 * q:4 * q + 4],
                                 in_=st2["rv"][:, 4 * q:4 * q + 4],
                                 func=AF.Sqrt).then_inc(S["sc_rstd2"], 1)
            s.wait_ge(S["v_lgsh"], 1)
            s.activation(out=rw[2][:], in_=rw[1][:],
                         func=AF.Exp).then_inc(S["sc_exp"], 1)
            NU = (e_run * 4 * 26) if LV >= 3 else 0

            def silu_emit(pm):
                psp = pm // 26
                pml, ptq = (pm // 2) % 13, pm % 2
                s.wait_ge(S["v_stt"], pm + 1)
                if pm % 26 == 0 and psp >= 1:
                    s.wait_ge(S["t_d"], 16 * psp)
                s.activation(out=hq[:, pml, ptq * 512:(ptq + 1) * 512],
                             in_=t1b[:, pm % 2, :],
                             func=AF.Silu).then_inc(S["sc_h"], 1)

            if NU > 0:
                s.wait_ge(S["s_cw"], 32)
            for u in range(NU):
                sp = u // 26
                e = sp // 4
                fh = (sp // 2) % 2
                ml, tq = (u // 2) % 13, u % 2
                if u >= 1:
                    silu_emit(u - 1)
                s.wait_ge(S["t_h"], u + 1)
                if u >= 2:
                    s.wait_ge(S["v_stt"], u - 1)
                s.activation(out=t1b[:, u % 2, :], in_=gp_ps[tq][:],
                             func=AF.Identity,
                             bias=gpbP[:, e, (fh * 13 + ml):(fh * 13 + ml) + 1]
                             ).then_inc(S["sc_t1"], 1)
            if NU > 0:
                silu_emit(NU - 1)

    ctx.close()
    return nc


def _prep(inputs):
    f32 = np.float32
    a = np.ascontiguousarray(inputs["L0_attn_a"], f32)
    b = np.ascontiguousarray(inputs["L0_attn_b"], f32)
    w_out = np.ascontiguousarray(inputs["L0_attn_w_out"], f32)
    b_out = np.ascontiguousarray(inputs["L0_attn_b_out"], f32)
    gate_w = np.ascontiguousarray(inputs["L0_ffn_gate_w"], f32)
    gate_b = np.ascontiguousarray(inputs["L0_ffn_gate_b"], f32)
    up_w = np.ascontiguousarray(inputs["L0_ffn_up_w"], f32)
    up_b = np.ascontiguousarray(inputs["L0_ffn_up_b"], f32)
    gp_w = np.ascontiguousarray(inputs["L0_ffn_gp_w"], f32)
    gp_b = np.ascontiguousarray(inputs["L0_ffn_gp_b"], f32)
    down_w = np.ascontiguousarray(inputs["L0_ffn_down_w"], f32)
    down_b = np.ascontiguousarray(inputs["L0_ffn_down_b"], f32)

    shared = {}
    shared["a_ft"] = np.ascontiguousarray(a.reshape(8, 128).T)
    shared["b_ft"] = np.ascontiguousarray(b.reshape(8, 128).T)
    shared["b_outB"] = np.ascontiguousarray(np.broadcast_to(b_out, (128, D)))
    shared["gate_bB"] = np.ascontiguousarray(np.broadcast_to(gate_b, (128, 8)))
    io = np.broadcast_to(np.arange(8, dtype=f32), (128, 8))
    shared["iota8"] = np.ascontiguousarray(io)
    shared["iota16"] = np.ascontiguousarray(io + 16.0)
    shared["woT"] = np.ascontiguousarray(w_out.T)
    shared["gateT"] = np.ascontiguousarray(gate_w.T)
    shared["down_b"] = down_b

    gpT = np.zeros((E, D, FP), f32)
    gpT[:, :, :F] = gp_w.transpose(0, 2, 1)
    upT = np.zeros((E, D, FP), f32)
    upT[:, :, :F] = up_w.transpose(0, 2, 1)
    dnT = np.zeros((E, FP, D), f32)
    dnT[:, :F, :] = down_w.transpose(0, 2, 1)
    shared["gpT"] = gpT
    shared["upT"] = upT
    shared["dnT"] = dnT

    def packb(bias):
        p = np.zeros((E, FP), f32)
        p[:, :F] = bias
        return np.ascontiguousarray(p.reshape(E, MF, 128).transpose(0, 2, 1))

    shared["gp_bP"] = packb(gp_b)
    shared["up_bP"] = packb(up_b)
    return shared


def kernel(L0_attn_x, L0_attn_a, L0_attn_b, L0_attn_w_out, L0_attn_b_out,
           L0_ffn_gate_w, L0_ffn_gate_b, L0_ffn_up_w, L0_ffn_up_b,
           L0_ffn_gp_w, L0_ffn_gp_b, L0_ffn_down_w, L0_ffn_down_b):
    inputs = dict(L0_attn_a=L0_attn_a, L0_attn_b=L0_attn_b,
                  L0_attn_w_out=L0_attn_w_out, L0_attn_b_out=L0_attn_b_out,
                  L0_ffn_gate_w=L0_ffn_gate_w, L0_ffn_gate_b=L0_ffn_gate_b,
                  L0_ffn_up_w=L0_ffn_up_w, L0_ffn_up_b=L0_ffn_up_b,
                  L0_ffn_gp_w=L0_ffn_gp_w, L0_ffn_gp_b=L0_ffn_gp_b,
                  L0_ffn_down_w=L0_ffn_down_w, L0_ffn_down_b=L0_ffn_down_b)
    shared = _prep(inputs)
    x = np.ascontiguousarray(np.asarray(L0_attn_x, np.float32))
    nc = build()
    in_maps = []
    for bcore in range(NCORE):
        m = dict(shared)
        m["x_in"] = np.ascontiguousarray(x[bcore])
        in_maps.append(m)
    res = run_bass_kernel_spmd(nc, in_maps, core_ids=list(range(NCORE)))
    out = np.stack([np.asarray(r["out"]) for r in res.results], axis=0)
    return out


# version 20 (final)
# speedup vs baseline: 1.0043x; 1.0043x over previous
import numpy as np
import ml_dtypes
from contextlib import ExitStack

import concourse.bass as bass
import concourse.mybir as mybir
from concourse.bass_utils import run_bass_kernel_spmd

F32 = mybir.dt.float32
BF16 = mybir.dt.bfloat16
AF = mybir.ActivationFunctionType
OP = mybir.AluOpType
AX = mybir.AxisListType

NCORE = 8
T = 2048
D = 1024
E = 8
F = 3264
FP = 3328
KD = 8
MF = 26
NT = 16
NQ = 4
EPS = 1e-5
BIG = 1e30
NPASS = E * NQ
NG = NPASS * MF


def build(dbg=False, stop="full", e_run=E):
    LV = {"ln": 0.3, "scan": 0.6, "bout": 0.8, "wonly": 0.9, "attn": 1,
          "routing": 2, "full": 3}[stop]
    nc = bass.Bass("TRN2", target_bir_lowering=False)

    x_d = nc.dram_tensor("x_in", [T, D], F32, kind="ExternalInput").ap()
    aft_d = nc.dram_tensor("a_ft", [128, 8], F32, kind="ExternalInput").ap()
    bft_d = nc.dram_tensor("b_ft", [128, 8], F32, kind="ExternalInput").ap()
    boutB_d = nc.dram_tensor("b_outB", [128, D], F32, kind="ExternalInput").ap()
    gbB_d = nc.dram_tensor("gate_bB", [128, 8], F32, kind="ExternalInput").ap()
    io8_d = nc.dram_tensor("iota8", [128, 8], F32, kind="ExternalInput").ap()
    io16_d = nc.dram_tensor("iota16", [128, 8], F32, kind="ExternalInput").ap()
    gpbP_d = nc.dram_tensor("gp_bP", [E, 128, MF], F32, kind="ExternalInput").ap()
    upbP_d = nc.dram_tensor("up_bP", [E, 128, MF], F32, kind="ExternalInput").ap()
    woT_d = nc.dram_tensor("woT", [D, D], F32, kind="ExternalInput").ap()
    gateT_d = nc.dram_tensor("gateT", [D, 8], F32, kind="ExternalInput").ap()
    dnb_d = nc.dram_tensor("down_b", [E, D], F32, kind="ExternalInput").ap()
    gpT_d = nc.dram_tensor("gpT", [E, D, FP], F32, kind="ExternalInput").ap()
    upT_d = nc.dram_tensor("upT", [E, D, FP], F32, kind="ExternalInput").ap()
    dnT_d = nc.dram_tensor("dnT", [E, FP, D], F32, kind="ExternalInput").ap()
    out_d = nc.dram_tensor("out", [T, D], F32, kind="ExternalOutput").ap()
    if dbg:
        dbg_xa = nc.dram_tensor("dbg_xa", [T, D], F32, kind="ExternalOutput").ap()
        dbg_comb = nc.dram_tensor("dbg_comb", [128, NT, 8], F32,
                                  kind="ExternalOutput").ap()
        dbg_y = nc.dram_tensor("dbg_y", [128, KD, T], BF16,
                               kind="ExternalOutput").ap()

    ctx = ExitStack()
    sb = lambda name, shape, dt: ctx.enter_context(nc.sbuf_tensor(name, shape, dt))
    psA = [ctx.enter_context(nc.psum_tensor(f"psA{i}", [128, 512], F32))
           for i in range(2)]
    psB = [ctx.enter_context(nc.psum_tensor(f"psB{i}", [128, 512], F32))
           for i in range(2)]
    gp_ps = [ctx.enter_context(nc.psum_tensor(f"gpps{i}", [128, 512], F32))
             for i in range(2)]
    up_ps = [ctx.enter_context(nc.psum_tensor(f"upps{i}", [128, 512], F32))
             for i in range(2)]
    d_banks = [psB[0], psB[1], psA[0], psA[1]]
    psC = [psA[0][:].bitcast(BF16), psA[1][:].bitcast(BF16)]
    psCb = psC + [gp_ps[0][:].bitcast(BF16), gp_ps[1][:].bitcast(BF16)]

    x_sb = sb("x_sb", [128, NT, D], F32)
    xn2T = sb("xn2T", [128, KD, T], BF16)
    hq = sb("hq", [128, 13, 1024], BF16)
    dnT_sb = sb("dnT_sb", [128, MF, D], BF16)
    gp_sb = sb("gp_sb", [128, 2, KD, 256], BF16)
    up_sb = sb("up_sb", [128, 2, KD, 256], BF16)
    aft = sb("aft", [128, 8], F32)
    ah = sb("ah", [128, 8], F32)
    bft = sb("bft", [128, 8], F32)
    gbB = sb("gbB", [128, 8], F32)
    io8 = sb("io8", [128, 8], F32)
    io16 = sb("io16", [128, 8], F32)
    gpbP = sb("gpbP", [128, E, MF], F32)
    upbP = sb("upbP", [128, E, MF], F32)
    dnb_sb = sb("dnb_sb", [8, D], BF16)
    gateT_sb = sb("gateT_sb", [128, KD, 8], BF16)
    identf = sb("identf", [128, 128], F32)
    identb = sb("identb", [128, 128], BF16)
    t1b = sb("t1b", [128, 2, 512], F32)
    combT_sb2 = sb("combT_sb2", [8, T], BF16)
    comb = sb("comb", [128, NT, 8], F32)
    comb_bf = sb("comb_bf", [128, NT, 8], BF16)
    st1 = {k: sb("st1_" + k, [128, NT], F32) for k in
           ["sum", "sq", "m", "msq", "var", "rv", "rstd", "nmr"]}
    st2 = {k: sb("st2_" + k, [128, NT], F32) for k in
           ["sum", "sq", "m", "msq", "var", "rv", "rstd", "nmr"]}
    rw = [sb("rw%d" % i, [128, NT, 8], F32) for i in range(4)]
    rtn = {k: sb("rtn_" + k, [128, NT, 1], F32) for k in
           ["nmx", "z", "rz", "m1", "m2", "idx", "den", "inv"]}

    r1f = hq[:].rearrange("p a b -> p (a b)").bitcast(F32)
    r1b = hq[:].rearrange("p a b -> p (a b)")
    bx_pp = [r1f[:, 0:2048], r1f[:, 2048:4096]]
    junk = r1f[:, 4608:5632]
    xn2_t = [r1b[:, 11264:12288], r1b[:, 12288:13312]]
    xn2T_f = xn2T[:].rearrange("p k t -> p (k t)").bitcast(F32)
    xn1_q = [xn2T_f[:, 512 * j:512 * (j + 1)] for j in range(16)]
    r2f = dnT_sb[:].rearrange("p a b -> p (a b)").bitcast(F32)
    r2b = dnT_sb[:].rearrange("p a b -> p (a b)")
    w_oT = r2b[:, 0:8192].rearrange("p (k e) -> p k e", k=KD)
    ybf = r2b[:, 8192:24576].rearrange("p (k t) -> p k t", k=KD)
    b_outB = r2f[:, 12288:13312]
    lg8 = r2f[0:8, 4096:6144]
    combT = combT_sb2[:]

    sem_names = ["s_x", "s_xg", "s_xv", "s_xs", "s_c", "s_cw", "s_bo", "s_wo",
                 "s_gt", "s_db", "s_dn",
                 "s_gpA", "s_gpB", "g_id", "g_ln", "g_bo",
                 "t_tr", "t_wo", "t_xtr", "t_gate", "t_ltr", "t_ctr",
                 "t_r8", "t_h", "t_d",
                 "v_ln", "v_bxe", "v_ybf", "v_woe", "v_xtile", "v_xne",
                 "v_gevac", "v_ltre", "v_comb", "v_ctre", "v_r8e",
                 "v_stt", "v_de", "v_ch", "v_eps1", "v_eps2", "v_lgsh",
                 "sc_sq", "sc_rstd", "sc_sq2", "sc_rstd2", "sc_exp", "sc_ah",
                 "sc_t1", "sc_h", "c_gu", "s_st", "s_dbg"]
    S = {}
    for n in sem_names:
        S[n] = ctx.enter_context(nc.semaphore(n))
    s_gp = [S["s_gpA"], S["s_gpB"]]

    DN_CH = [(0, 7), (7, 13), (13, 20), (20, 26)]
    NSP = E * 4

    def slice_mrange(mp):
        return (2 * mp, min(2 * mp + 2, 13))

    def slice_consumed_th(sidx):
        sp2, mp2 = sidx // 7, sidx % 7
        mlast = min(2 * mp2 + 1, 12)
        return (sp2 * 13 + mlast) * 2 + 2

    xq_d = x_d.rearrange("(t p) d -> p t d", p=128)

    with nc.Block() as block:

        @block.sync
        def _(sy):
            for src, dst in [(aft_d, aft), (bft_d, bft), (gbB_d, gbB),
                             (io8_d, io8), (io16_d, io16)]:
                sy.dma_start(out=dst[:], in_=src).then_inc(S["s_c"], 16)
            sy.dma_start(out=x_sb[:, 0:2, :], in_=xq_d[:, 0:2, :]
                         ).then_inc(S["s_x"], 16)
            sy.dma_start(out=x_sb[:, 6:8, :], in_=xq_d[:, 6:8, :]
                         ).then_inc(S["s_x"], 16)
            sy.dma_start(out=x_sb[:, 12:14, :], in_=xq_d[:, 12:14, :]
                         ).then_inc(S["s_x"], 16)
            sy.dma_start(out=gpbP[:], in_=gpbP_d.rearrange("e p m -> p e m")
                         ).then_inc(S["s_cw"], 16)
            sy.dma_start(out=upbP[:], in_=upbP_d.rearrange("e p m -> p e m")
                         ).then_inc(S["s_cw"], 16)
            sy.dma_start(out=b_outB, in_=boutB_d).then_inc(S["s_bo"], 16)
            if dbg:
                sy.wait_ge(S["v_ybf"], 4 * KD)
                sy.dma_start(out=dbg_y, in_=ybf).then_inc(S["s_dbg"], 16)
                sy.wait_ge(S["v_woe"], 32)
                sy.dma_start(out=dbg_xa.rearrange("(t p) d -> p t d", p=128),
                             in_=x_sb[:]).then_inc(S["s_dbg"], 16)
                sy.wait_ge(S["v_comb"], 1)
                sy.dma_start(out=dbg_comb, in_=comb[:]).then_inc(S["s_dbg"], 16)
            for g in range(NT):
                if LV < 3:
                    if LV == 0.3:
                        sy.wait_ge(S["sc_rstd"], 4)
                    elif LV == 0.6:
                        sy.wait_ge(S["v_ybf"], 4 * KD)
                    elif LV == 0.8:
                        sy.wait_ge(S["v_ch"], 1)
                    elif LV == 0.9:
                        sy.wait_ge(S["t_wo"], 32)
                    elif LV == 1:
                        sy.wait_ge(S["v_woe"], 32)
                    elif LV == 2:
                        sy.wait_ge(S["v_r8e"], 32)
                    sy.dma_start(
                        out=out_d.rearrange("(t p) d -> p t d", p=128)[:, g, :],
                        in_=x_sb[:, g, :]).then_inc(S["s_st"], 16)
                    continue
                spf = 4 * (e_run - 1) + 2 + g // 8
                for dh in range(2):
                    sy.wait_ge(S["v_de"], spf * 16 + (g % 8) * 2 + dh + 1)
                    sy.dma_start(
                        out=out_d.rearrange("(t p) d -> p t d", p=128)[
                            :, g, dh * 512:(dh + 1) * 512],
                        in_=x_sb[:, g, dh * 512:(dh + 1) * 512]
                        ).then_inc(S["s_st"], 16)
            sy.wait_ge(S["s_st"], 16 * NT * (1 if LV < 3 else 2))

        @block.gpsimd
        def _(g):
            g.memset(identf[:], 0.0).then_inc(S["g_id"], 1)
            g.wait_ge(S["g_id"], 1)
            g.affine_select(out=identf[:], in_=identf[:], compare_op=OP.not_equal,
                            fill=1.0, base=0, pattern=[[-1, 128]],
                            channel_multiplier=1).then_inc(S["g_id"], 1)
            g.memset(identb[:], 0.0).then_inc(S["g_id"], 1)
            g.wait_ge(S["g_id"], 3)
            g.affine_select(out=identb[:], in_=identb[:], compare_op=OP.not_equal,
                            fill=1.0, base=0, pattern=[[-1, 128]],
                            channel_multiplier=1).then_inc(S["g_id"], 1)
            g.dma_start(out=x_sb[:, 4:6, :], in_=xq_d[:, 4:6, :]
                        ).then_inc(S["s_xg"], 16)
            g.dma_start(out=x_sb[:, 10:12, :], in_=xq_d[:, 10:12, :]
                        ).then_inc(S["s_xg"], 16)
            g.dma_start(out=gateT_sb[:],
                        in_=gateT_d.rearrange("(k p) e -> p k e", p=128)
                        ).then_inc(S["s_gt"], 16)
            g.dma_start(out=w_oT, in_=woT_d.rearrange("(k p) e -> p k e", p=128)
                        ).then_inc(S["s_wo"], 16)
            g.dma_start(out=dnb_sb[:], in_=dnb_d).then_inc(S["s_db"], 16)

            def emit_slice(sp, mp):
                e, fh = sp // 4, (sp // 2) % 2
                sidx = sp * 7 + mp
                buf = sidx % 2
                if sidx >= 2:
                    g.wait_ge(S["t_h"], slice_consumed_th(sidx - 2))
                m0, m1 = slice_mrange(mp)
                c0 = (fh * 13 + m0) * 128
                c1 = (fh * 13 + m1) * 128
                w = c1 - c0
                g.dma_start(
                    out=gp_sb[:, buf, :, 0:w],
                    in_=gpT_d[e, :, c0:c1].rearrange(
                        "(k p) f -> p k f", p=128)).then_inc(s_gp[buf], 16)
                g.dma_start(
                    out=up_sb[:, buf, :, 0:w],
                    in_=upT_d[e, :, c0:c1].rearrange(
                        "(k p) f -> p k f", p=128)).then_inc(s_gp[buf], 16)

            def emit_dnt(e, ci):
                if e == 0:
                    if ci == 1:
                        g.wait_ge(S["t_ltr"], NT)
                else:
                    g.wait_ge(S["t_d"],
                              64 * (e - 1) + (32 if ci in (0, 1) else 64))
                k0, k1 = DN_CH[ci]
                g.dma_start(
                    out=dnT_sb[:, k0:k1, :],
                    in_=dnT_d[e, k0 * 128:k1 * 128, :].rearrange(
                        "(k p) d -> p k d", p=128)).then_inc(S["s_dn"], 16)

            if LV >= 3 and e_run > 0:
                emit_slice(0, 0)
                emit_slice(0, 1)

            for e in range(e_run if LV >= 3 else 0):
                sp0 = e * 4
                for mp in range(2 if e == 0 else 0, 4):
                    emit_slice(sp0, mp)
                if e == 0:
                    g.wait_ge(S["v_woe"], 32)
                    if dbg:
                        g.wait_ge(S["s_dbg"], 48)
                emit_dnt(e, 0)
                emit_slice(sp0, 4)
                emit_dnt(e, 1)
                emit_slice(sp0, 5)
                emit_dnt(e, 2)
                emit_slice(sp0, 6)
                emit_dnt(e, 3)
                for sp in range(sp0 + 1, sp0 + 4):
                    for mp in range(7):
                        emit_slice(sp, mp)

        @block.tensor
        def _(t):
            t.wait_ge(S["g_id"], 4)

            def trbank(g):
                return (psA[g % 2] if g >= 28
                        else [psA[0], psA[1], psB[0], psB[1]][g % 4])
            for g in range(32 if LV >= 0.6 else 0):
                for p2 in range(2):
                    pr = 2 * g + p2
                    c = (2 * pr) // 16
                    t.wait_ge(S["v_ln"], (c // 4) * 16 + (2 * pr) % 16 + 2)
                    if p2 == 0:
                        if g >= 28:
                            t.wait_ge(S["v_bxe"], g - 1)
                        elif g >= 4:
                            t.wait_ge(S["v_bxe"], g - 3)
                    for j in range(2):
                        a = 2 * pr + j
                        ins = t.transpose(
                            trbank(g)[:, (a % 4) * 128:(a % 4 + 1) * 128],
                            xn1_q[a % 16][:, (c % 4) * 128:(c % 4 + 1) * 128],
                            identf[:])
                    ins.then_inc(S["t_tr"], 1)
            if LV < 0.9:
                return
            t.wait_ge(S["s_wo"], 16)
            t.wait_ge(S["v_bxe"], 28)
            for m in range(NT):
                for eh in range(2):
                    widx = m * 2 + eh
                    if widx >= 2:
                        t.wait_ge(S["v_woe"] if LV >= 1 else S["t_wo"], widx - 1)
                    for k in range(KD):
                        if widx < 6:
                            t.wait_ge(S["v_ybf"], 4 * (k + 1))
                        ins = t.matmul(psB[widx % 2][:],
                                       ybf[:, k, m * 128:(m + 1) * 128],
                                       w_oT[:, k, eh * 512:(eh + 1) * 512],
                                       start=(k == 0), stop=(k == KD - 1))
                    ins.then_inc(S["t_wo"], 1)
            if LV < 2:
                return
            t.wait_ge(S["s_gt"], 16)
            t.wait_ge(S["v_woe"], 32)
            for tt in range(NT):
                t.wait_ge(S["v_xtile"], tt + 1)
                if tt >= 4:
                    t.wait_ge(S["v_xne"], tt - 3)
                for c in range(KD):
                    ins = t.transpose(psCb[tt % 4][:, c * 128:(c + 1) * 128],
                                      xn2_t[tt % 2][:, c * 128:(c + 1) * 128],
                                      identb[:])
                ins.then_inc(S["t_xtr"], 1)
                if tt % 4 == 3:
                    tq = tt // 4
                    t.wait_ge(S["v_xne"], 4 * tq + 4)
                    if tq >= 2:
                        t.wait_ge(S["v_gevac"], tq - 1)
                    for k in range(KD):
                        ins = t.matmul(psB[tq % 2][0:8, :], gateT_sb[:, k, :],
                                       xn2T[:, k, tq * 512:(tq + 1) * 512],
                                       start=(k == 0), stop=(k == KD - 1))
                    ins.then_inc(S["t_gate"], 1)
            t.wait_ge(S["v_xne"], NT)
            t.wait_ge(S["v_bxe"], 32)
            for g in range(NT):
                t.wait_ge(S["v_gevac"], g // 4 + 1)
                if g >= 2:
                    t.wait_ge(S["v_ltre"], g - 1)
                t.transpose(psA[g % 2][:, 0:8], lg8[:, g * 128:(g + 1) * 128],
                            identf[0:8, 0:8]).then_inc(S["t_ltr"], 1)

            def emit_ctr_r8():
                t.wait_ge(S["v_comb"], 1)
                t.wait_ge(S["v_ltre"], NT)
                for g in range(NT):
                    if g >= 2:
                        t.wait_ge(S["v_ctre"], g - 1)
                    t.transpose(psC[g % 2][0:8, 0:128], comb_bf[:, g, :],
                                identb[:]).then_inc(S["t_ctr"], 1)
                t.wait_ge(S["v_ctre"], NT)
                t.wait_ge(S["s_db"], 16)
                for g in range(NT):
                    for dh in range(2):
                        ridx = g * 2 + dh
                        if ridx >= 2:
                            t.wait_ge(S["v_r8e"], ridx - 1)
                        t.matmul(psA[ridx % 2][:],
                                 combT[:, g * 128:(g + 1) * 128],
                                 dnb_sb[:, dh * 512:(dh + 1) * 512],
                                 start=True, stop=True).then_inc(S["t_r8"], 1)
            if LV == 2:
                emit_ctr_r8()
            for sp in range(NSP if LV >= 3 else 0):
                if sp >= e_run * 4:
                    break
                e, th, fh = sp // 4, sp % 2, (sp // 2) % 2
                for m in range(13):
                    mg = sp * 13 + m
                    mp = m // 2
                    sidx = sp * 7 + mp
                    buf = sidx % 2
                    if m % 2 == 0:
                        t.wait_ge(s_gp[buf], 32 * (sidx // 2 + 1))
                    fo = (m % 2) * 128
                    for tq in range(2):
                        u = mg * 2 + tq
                        if u >= 2:
                            t.wait_ge(S["v_stt"], u - 1)
                        for k in range(KD):
                            t.matmul(gp_ps[tq][:],
                                     gp_sb[:, buf, k, fo:fo + 128],
                                     xn2T[:, k, (th * 2 + tq) * 512:
                                          (th * 2 + tq + 1) * 512],
                                     start=(k == 0), stop=(k == KD - 1))
                            ins = t.matmul(up_ps[tq][:],
                                           up_sb[:, buf, k, fo:fo + 128],
                                           xn2T[:, k, (th * 2 + tq) * 512:
                                                (th * 2 + tq + 1) * 512],
                                           start=(k == 0), stop=(k == KD - 1))
                        ins.then_inc(S["t_h"], 1)
                if sp == 0:
                    emit_ctr_r8()
                    t.wait_ge(S["v_r8e"], 32)
                    t.wait_ge(S["v_gevac"], NQ)
                if sp % 4 == 0:
                    t.wait_ge(S["s_dn"], 64 * e + 32)
                elif sp % 4 == 2:
                    t.wait_ge(S["s_dn"], 64 * e + 64)
                for tc in range(8):
                    didx = sp * 16 + tc * 2
                    if didx >= 4:
                        t.wait_ge(S["v_de"], didx - 2)
                    bankA = d_banks[didx % 4]
                    bankB = d_banks[(didx + 1) % 4]
                    for k in range(13):
                        if tc == 0 and k % 2 == 0:
                            t.wait_ge(S["sc_h"], 26 * sp + min(2 * k + 4, 26))
                        insA = t.matmul(bankA[:],
                                        hq[:, k, tc * 128:(tc + 1) * 128],
                                        dnT_sb[:, fh * 13 + k, 0:512],
                                        start=(k == 0), stop=(k == 12))
                        insB = t.matmul(bankB[:],
                                        hq[:, k, tc * 128:(tc + 1) * 128],
                                        dnT_sb[:, fh * 13 + k, 512:1024],
                                        start=(k == 0), stop=(k == 12))
                    insA.then_inc(S["t_d"], 1)
                    insB.then_inc(S["t_d"], 1)

        @block.vector
        def _(v):
            vch = [0]

            def vsync(ins):
                vch[0] += 1
                ins.then_inc(S["v_ch"], 1)
                v.wait_ge(S["v_ch"], vch[0])

            v.wait_ge(S["s_c"], 80)
            v.wait_ge(S["sc_ah"], 1)

            def stats1_q(q):
                sl = np.s_[:, 4 * q:4 * q + 4]
                v.wait_ge(S["sc_sq"], 8 * (q + 1))
                vsync(v.tensor_scalar(st1["m"][sl], st1["sum"][sl], -1.0 / D,
                                      None, OP.mult))
                vsync(v.tensor_tensor(st1["msq"][sl], st1["m"][sl],
                                      st1["m"][sl], OP.mult))
                vsync(v.scalar_tensor_tensor(st1["var"][sl], st1["sq"][sl],
                                             1.0 / D, st1["msq"][sl],
                                             OP.mult, OP.subtract))
                vsync(v.tensor_scalar(st1["var"][sl], st1["var"][sl], EPS,
                                      None, OP.add))
                v.reciprocal(st1["rv"][sl], st1["var"][sl]
                             ).then_inc(S["v_eps1"], 1)

            if LV < 0.6:
                for q in range(4):
                    stats1_q(q)
                return

            trbanksv = [psA[0], psA[1], psB[0], psB[1]]

            def scan_piece(c, q):
                init = 0.0 if q == 0 else ybf[:, c, 512 * q - 1:512 * q]
                v.tensor_tensor_scan(ybf[:, c, 512 * q:512 * (q + 1)],
                                     ah[:, c:c + 1].to_broadcast([128, 512]),
                                     bx_pp[c % 2][:, 512 * q:512 * (q + 1)],
                                     init, OP.mult,
                                     OP.add).then_inc(S["v_ybf"], 1)

            def emit_evac(pg):
                c = pg // 4
                bank = (psA[pg % 2] if pg >= 28 else trbanksv[pg % 4])
                v.wait_ge(S["t_tr"], 2 * pg + 2)
                v.tensor_scalar(bx_pp[c % 2][:, (pg % 4) * 512:
                                             (pg % 4 + 1) * 512],
                                bank[:, 0:512], bft[:, c:c + 1],
                                None, OP.mult).then_inc(S["v_bxe"], 1)
                scan_piece(c, pg % 4)

            for pg in range(32):
                if pg < 4:
                    stats1_q(pg)
                emit_evac(pg)
            if LV < 0.8:
                return
            v.wait_ge(S["s_bo"], 16)
            for tt in range(NT):
                ins = v.tensor_tensor(x_sb[:, tt, :], x_sb[:, tt, :], b_outB,
                                      OP.add)
            vsync(ins)
            if LV < 1:
                return
            for widx in range(32):
                v.wait_ge(S["t_wo"], widx + 1)
                m, eh = widx // 2, widx % 2
                v.tensor_tensor(x_sb[:, m, eh * 512:(eh + 1) * 512],
                                x_sb[:, m, eh * 512:(eh + 1) * 512],
                                psB[widx % 2][:], OP.add).then_inc(S["v_woe"], 1)
            if LV < 2:
                return

            def stats2_q(q):
                sl = np.s_[:, 4 * q:4 * q + 4]
                v.wait_ge(S["sc_sq2"], 8 * (q + 1))
                vsync(v.tensor_scalar(st2["m"][sl], st2["sum"][sl], -1.0 / D,
                                      None, OP.mult))
                vsync(v.tensor_tensor(st2["msq"][sl], st2["m"][sl],
                                      st2["m"][sl], OP.mult))
                vsync(v.scalar_tensor_tensor(st2["var"][sl], st2["sq"][sl],
                                             1.0 / D, st2["msq"][sl],
                                             OP.mult, OP.subtract))
                vsync(v.tensor_scalar(st2["var"][sl], st2["var"][sl], EPS,
                                      None, OP.add))
                v.reciprocal(st2["rv"][sl], st2["var"][sl]
                             ).then_inc(S["v_eps2"], 1)

            def gevac(tq):
                v.wait_ge(S["t_gate"], tq + 1)
                v.tensor_copy(lg8[:, tq * 512:(tq + 1) * 512],
                              psB[tq % 2][0:8, :]).then_inc(S["v_gevac"], 1)

            def xn2_copy(pt):
                v.wait_ge(S["t_xtr"], pt + 1)
                v.tensor_copy(xn2T[:, :, pt * 128:(pt + 1) * 128],
                              psCb[pt % 4].rearrange("p (k f) -> p k f", k=KD)
                              ).then_inc(S["v_xne"], 1)
                if pt >= 5 and (pt - 5) % 4 == 0 and pt <= 13:
                    gevac((pt - 5) // 4)

            for tt in range(NT):
                if tt % 4 == 0:
                    stats2_q(tt // 4)
                v.wait_ge(S["sc_rstd2"], tt // 4 + 1)
                if tt >= 2:
                    v.wait_ge(S["t_xtr"], tt - 1)
                v.tensor_scalar(xn2_t[tt % 2], x_sb[:, tt, :],
                                st2["m"][:, tt:tt + 1], st2["rstd"][:, tt:tt + 1],
                                OP.add, OP.mult).then_inc(S["v_xtile"], 1)
                if tt >= 1:
                    xn2_copy(tt - 1)
            xn2_copy(NT - 1)
            gevac(3)
            for g in range(NT):
                v.wait_ge(S["t_ltr"], g + 1)
                v.tensor_tensor(rw[0][:, g, :], psA[g % 2][:, 0:8], gbB[:],
                                OP.add).then_inc(S["v_ltre"], 1)
            v.wait_ge(S["v_ltre"], NT)
            WB = [128, NT, 8]
            io8B = io8[:, None, :].to_broadcast(WB)
            io16B = io16[:, None, :].to_broadcast(WB)
            vsync(v.tensor_reduce(rtn["nmx"][:], rw[0][:], axis=AX.X, op=OP.max,
                                  negate=True))
            v.tensor_tensor(rw[1][:], rw[0][:],
                            rtn["nmx"][:].to_broadcast(WB),
                            OP.add).then_inc(S["v_lgsh"], 1)
            v.wait_ge(S["sc_exp"], 1)
            vsync(v.tensor_reduce(rtn["z"][:], rw[2][:], axis=AX.X, op=OP.add))
            vsync(v.reciprocal(rtn["rz"][:], rtn["z"][:]))
            vsync(v.tensor_tensor(rw[1][:], rw[2][:],
                                  rtn["rz"][:].to_broadcast(WB), OP.mult))
            vsync(v.tensor_reduce(rtn["m1"][:], rw[1][:], axis=AX.X, op=OP.max))
            vsync(v.tensor_tensor(rw[0][:], rw[1][:],
                                  rtn["m1"][:].to_broadcast(WB),
                                  OP.is_equal))
            vsync(v.scalar_tensor_tensor(rw[3][:], rw[0][:], -16.0,
                                         io16B, OP.mult, OP.add))
            vsync(v.tensor_reduce(rtn["idx"][:], rw[3][:], axis=AX.X,
                                  op=OP.min))
            vsync(v.tensor_tensor(rw[0][:], io8B,
                                  rtn["idx"][:].to_broadcast(WB),
                                  OP.is_equal))
            vsync(v.scalar_tensor_tensor(rw[3][:], rw[0][:], -BIG,
                                         rw[1][:], OP.mult, OP.add))
            vsync(v.tensor_reduce(rtn["m2"][:], rw[3][:], axis=AX.X, op=OP.max))
            vsync(v.tensor_tensor(rw[2][:], rw[3][:],
                                  rtn["m2"][:].to_broadcast(WB),
                                  OP.is_equal))
            vsync(v.scalar_tensor_tensor(rw[1][:], rw[2][:], -16.0,
                                         io16B, OP.mult, OP.add))
            vsync(v.tensor_reduce(rtn["idx"][:], rw[1][:], axis=AX.X,
                                  op=OP.min))
            vsync(v.tensor_tensor(rw[2][:], io8B,
                                  rtn["idx"][:].to_broadcast(WB),
                                  OP.is_equal))
            vsync(v.tensor_tensor(rtn["den"][:], rtn["m1"][:], rtn["m2"][:], OP.add))
            vsync(v.tensor_scalar(rtn["den"][:], rtn["den"][:], 1e-20, None, OP.add))
            vsync(v.reciprocal(rtn["inv"][:], rtn["den"][:]))
            vsync(v.tensor_tensor(rw[1][:], rw[0][:],
                                  rtn["m1"][:].to_broadcast(WB), OP.mult))
            vsync(v.tensor_tensor(rw[2][:], rw[2][:],
                                  rtn["m2"][:].to_broadcast(WB), OP.mult))
            vsync(v.tensor_tensor(comb[:], rw[1][:], rw[2][:], OP.add))
            vsync(v.tensor_tensor(comb[:], comb[:],
                                  rtn["inv"][:].to_broadcast(WB), OP.mult))
            v.tensor_copy(comb_bf[:], comb[:]).then_inc(S["v_comb"], 1)

            def emit_ctr_r8_evacs():
                for g in range(NT):
                    v.wait_ge(S["t_ctr"], g + 1)
                    v.tensor_copy(combT[:, g * 128:(g + 1) * 128],
                                  psC[g % 2][0:8, 0:128]).then_inc(S["v_ctre"], 1)
                for ridx in range(32):
                    v.wait_ge(S["t_r8"], ridx + 1)
                    gg, dh = ridx // 2, ridx % 2
                    v.tensor_tensor(x_sb[:, gg, dh * 512:(dh + 1) * 512],
                                    x_sb[:, gg, dh * 512:(dh + 1) * 512],
                                    psA[ridx % 2][:], OP.add).then_inc(S["v_r8e"], 1)
            if LV == 2:
                emit_ctr_r8_evacs()
            if LV >= 3 and e_run > 0:
                v.wait_ge(S["s_cw"], 32)
            for sp in range(NSP if LV >= 3 else 0):
                if sp >= e_run * 4:
                    break
                e, th, fh = sp // 4, sp % 2, (sp // 2) % 2
                for m in range(13):
                    for tq in range(2):
                        u = (sp * 13 + m) * 2 + tq
                        v.wait_ge(S["sc_t1"], u + 1)
                        v.wait_ge(S["t_h"], u + 1)
                        if u >= 2:
                            v.wait_ge(S["sc_h"], u - 1)
                        v.scalar_tensor_tensor(
                            t1b[:, u % 2, :], up_ps[tq][:],
                            upbP[:, e, (fh * 13 + m):(fh * 13 + m) + 1],
                            t1b[:, u % 2, :],
                            OP.add, OP.mult).then_inc(S["v_stt"], 1)
                if sp == 0:
                    emit_ctr_r8_evacs()
                for tc in range(8):
                    for dh in range(2):
                        didx = sp * 16 + tc * 2 + dh
                        v.wait_ge(S["t_d"], didx + 1)
                        gt = th * 8 + tc
                        v.scalar_tensor_tensor(
                            x_sb[:, gt, dh * 512:(dh + 1) * 512],
                            d_banks[didx % 4][:],
                            comb[:, gt, e:e + 1],
                            x_sb[:, gt, dh * 512:(dh + 1) * 512],
                            OP.mult, OP.add).then_inc(S["v_de"], 1)

        @block.scalar
        def _(s):
            s.dma_start(out=x_sb[:, 2:4, :], in_=xq_d[:, 2:4, :]
                        ).then_inc(S["s_xs"], 16)
            s.dma_start(out=x_sb[:, 8:10, :], in_=xq_d[:, 8:10, :]
                        ).then_inc(S["s_xs"], 16)
            s.dma_start(out=x_sb[:, 14:16, :], in_=xq_d[:, 14:16, :]
                        ).then_inc(S["s_xs"], 16)
            s.wait_ge(S["s_c"], 80)
            s.activation(out=ah[:], in_=aft[:], func=AF.Tanh).then_inc(S["sc_ah"], 1)
            xw = {0: ("s_x", 16), 2: ("s_xs", 16), 4: ("s_xg", 16),
                  6: ("s_x", 32), 8: ("s_xs", 32), 10: ("s_xg", 32),
                  12: ("s_x", 48), 14: ("s_xs", 48)}
            def apply1(cp, tt):
                if cp >= 1 and tt % 2 == 0:
                    p = ((4 * cp - 1) * 16 + tt) // 2
                    s.wait_ge(S["t_tr"], p + 1)
                s.activation(out=xn1_q[tt],
                             in_=x_sb[:, tt, 512 * cp:512 * (cp + 1)],
                             func=AF.Identity,
                             scale=st1["rstd"][:, tt:tt + 1],
                             bias=st1["nmr"][:, tt:tt + 1]
                             ).then_inc(S["v_ln"], 1)

            for q in range(4):
                for tt in range(4 * q, 4 * q + 4):
                    if tt % 2 == 0:
                        sm, cnt = xw[tt]
                        s.wait_ge(S[sm], cnt)
                    s.activation(out=junk, in_=x_sb[:, tt, :], func=AF.Square,
                                 accum_out=st1["sq"][:, tt:tt + 1]
                                 ).then_inc(S["sc_sq"], 1)
                    s.activation(out=junk, in_=x_sb[:, tt, :], func=AF.Identity,
                                 accum_out=st1["sum"][:, tt:tt + 1]
                                 ).then_inc(S["sc_sq"], 1)
                s.wait_ge(S["v_eps1"], q + 1)
                s.activation(out=st1["rstd"][:, 4 * q:4 * q + 4],
                             in_=st1["rv"][:, 4 * q:4 * q + 4],
                             func=AF.Sqrt).then_inc(S["sc_rstd"], 1)
                if LV >= 0.6:
                    for tt in range(4 * q, 4 * q + 4):
                        s.activation(out=st1["nmr"][:, tt:tt + 1],
                                     in_=st1["m"][:, tt:tt + 1],
                                     func=AF.Identity,
                                     scale=st1["rstd"][:, tt:tt + 1])
                    for tt in range(4 * q, 4 * q + 4):
                        apply1(0, tt)
            if LV >= 0.6:
                for tt in range(NT):
                    apply1(1, tt)
            if LV < 2:
                return
            for tt in range(NT):
                s.wait_ge(S["v_woe"], 2 * tt + 2)
                s.activation(out=junk, in_=x_sb[:, tt, :], func=AF.Square,
                             accum_out=st2["sq"][:, tt:tt + 1]
                             ).then_inc(S["sc_sq2"], 1)
                s.activation(out=junk, in_=x_sb[:, tt, :], func=AF.Identity,
                             accum_out=st2["sum"][:, tt:tt + 1]
                             ).then_inc(S["sc_sq2"], 1)
                if tt % 4 == 3:
                    q = tt // 4
                    s.wait_ge(S["v_eps2"], q + 1)
                    s.activation(out=st2["rstd"][:, 4 * q:4 * q + 4],
                                 in_=st2["rv"][:, 4 * q:4 * q + 4],
                                 func=AF.Sqrt).then_inc(S["sc_rstd2"], 1)
            s.wait_ge(S["v_lgsh"], 1)
            s.activation(out=rw[2][:], in_=rw[1][:],
                         func=AF.Exp).then_inc(S["sc_exp"], 1)
            NU = (e_run * 4 * 26) if LV >= 3 else 0

            def silu_emit(pm):
                psp = pm // 26
                pml, ptq = (pm // 2) % 13, pm % 2
                s.wait_ge(S["v_stt"], pm + 1)
                if pm % 26 == 0 and psp >= 1:
                    s.wait_ge(S["t_d"], 16 * psp)
                s.activation(out=hq[:, pml, ptq * 512:(ptq + 1) * 512],
                             in_=t1b[:, pm % 2, :],
                             func=AF.Silu).then_inc(S["sc_h"], 1)

            if NU > 0:
                s.wait_ge(S["s_cw"], 32)
            for u in range(NU):
                sp = u // 26
                e = sp // 4
                fh = (sp // 2) % 2
                ml, tq = (u // 2) % 13, u % 2
                if u >= 1:
                    silu_emit(u - 1)
                s.wait_ge(S["t_h"], u + 1)
                if u >= 2:
                    s.wait_ge(S["v_stt"], u - 1)
                s.activation(out=t1b[:, u % 2, :], in_=gp_ps[tq][:],
                             func=AF.Identity,
                             bias=gpbP[:, e, (fh * 13 + ml):(fh * 13 + ml) + 1]
                             ).then_inc(S["sc_t1"], 1)
            if NU > 0:
                silu_emit(NU - 1)

    ctx.close()
    return nc


def _prep(inputs):
    f32 = np.float32
    a = np.ascontiguousarray(inputs["L0_attn_a"], f32)
    b = np.ascontiguousarray(inputs["L0_attn_b"], f32)
    w_out = np.ascontiguousarray(inputs["L0_attn_w_out"], f32)
    b_out = np.ascontiguousarray(inputs["L0_attn_b_out"], f32)
    gate_w = np.ascontiguousarray(inputs["L0_ffn_gate_w"], f32)
    gate_b = np.ascontiguousarray(inputs["L0_ffn_gate_b"], f32)
    up_w = np.ascontiguousarray(inputs["L0_ffn_up_w"], f32)
    up_b = np.ascontiguousarray(inputs["L0_ffn_up_b"], f32)
    gp_w = np.ascontiguousarray(inputs["L0_ffn_gp_w"], f32)
    gp_b = np.ascontiguousarray(inputs["L0_ffn_gp_b"], f32)
    down_w = np.ascontiguousarray(inputs["L0_ffn_down_w"], f32)
    down_b = np.ascontiguousarray(inputs["L0_ffn_down_b"], f32)

    shared = {}
    shared["a_ft"] = np.ascontiguousarray(a.reshape(8, 128).T)
    shared["b_ft"] = np.ascontiguousarray(b.reshape(8, 128).T)
    shared["b_outB"] = np.ascontiguousarray(np.broadcast_to(b_out, (128, D)))
    shared["gate_bB"] = np.ascontiguousarray(np.broadcast_to(gate_b, (128, 8)))
    io = np.broadcast_to(np.arange(8, dtype=f32), (128, 8))
    shared["iota8"] = np.ascontiguousarray(io)
    shared["iota16"] = np.ascontiguousarray(io + 16.0)
    shared["woT"] = np.ascontiguousarray(w_out.T)
    shared["gateT"] = np.ascontiguousarray(gate_w.T)
    shared["down_b"] = down_b

    gpT = np.zeros((E, D, FP), f32)
    gpT[:, :, :F] = gp_w.transpose(0, 2, 1)
    upT = np.zeros((E, D, FP), f32)
    upT[:, :, :F] = up_w.transpose(0, 2, 1)
    dnT = np.zeros((E, FP, D), f32)
    dnT[:, :F, :] = down_w.transpose(0, 2, 1)
    shared["gpT"] = gpT
    shared["upT"] = upT
    shared["dnT"] = dnT

    def packb(bias):
        p = np.zeros((E, FP), f32)
        p[:, :F] = bias
        return np.ascontiguousarray(p.reshape(E, MF, 128).transpose(0, 2, 1))

    shared["gp_bP"] = packb(gp_b)
    shared["up_bP"] = packb(up_b)
    return shared


def kernel(L0_attn_x, L0_attn_a, L0_attn_b, L0_attn_w_out, L0_attn_b_out,
           L0_ffn_gate_w, L0_ffn_gate_b, L0_ffn_up_w, L0_ffn_up_b,
           L0_ffn_gp_w, L0_ffn_gp_b, L0_ffn_down_w, L0_ffn_down_b):
    inputs = dict(L0_attn_a=L0_attn_a, L0_attn_b=L0_attn_b,
                  L0_attn_w_out=L0_attn_w_out, L0_attn_b_out=L0_attn_b_out,
                  L0_ffn_gate_w=L0_ffn_gate_w, L0_ffn_gate_b=L0_ffn_gate_b,
                  L0_ffn_up_w=L0_ffn_up_w, L0_ffn_up_b=L0_ffn_up_b,
                  L0_ffn_gp_w=L0_ffn_gp_w, L0_ffn_gp_b=L0_ffn_gp_b,
                  L0_ffn_down_w=L0_ffn_down_w, L0_ffn_down_b=L0_ffn_down_b)
    shared = _prep(inputs)
    x = np.ascontiguousarray(np.asarray(L0_attn_x, np.float32))
    nc = build()
    in_maps = []
    for bcore in range(NCORE):
        m = dict(shared)
        m["x_in"] = np.ascontiguousarray(x[bcore])
        in_maps.append(m)
    res = run_bass_kernel_spmd(nc, in_maps, core_ids=list(range(NCORE)))
    out = np.stack([np.asarray(r["out"]) for r in res.results], axis=0)
    return out
